# version 04
# baseline (speedup 1.0000x reference)
import math
from contextlib import ExitStack

import numpy as np
import ml_dtypes

import concourse.bass as bass
import concourse.mybir as mybir
from concourse.bass_utils import run_bass_kernel_spmd

F32 = mybir.dt.float32
BF16 = mybir.dt.bfloat16
I32 = mybir.dt.int32
AF = mybir.ActivationFunctionType
ALU = mybir.AluOpType
AX = mybir.AxisListType.X

D = 2048
T = 2048
NCD = 16
NH = 16
NKV = 4
DH = 128
IF = 8192
NIC = 64
NQB = 4
QB = 512
NTT = 16
NKT = 32
SK = 4096
SCALE = 1.0 / math.sqrt(DH)
EPS = 1e-5
BF = ml_dtypes.bfloat16


class E:

    def __init__(self, nc, ctx):
        self.nc, self.ctx = nc, ctx
        self.sems, self.cnt = {}, {}

    def s(self, name):
        if name not in self.sems:
            self.sems[name] = self.ctx.enter_context(self.nc.semaphore(name))
            self.cnt[name] = 0
        return self.sems[name]

    def inc(self, instr, name, n=1):
        instr.then_inc(self.s(name), n)
        self.cnt[name] += n
        return self.cnt[name]

    def v(self, name):
        self.s(name)
        return self.cnt[name]

    def wait_now(self, eng, *names):
        for nm in names:
            if self.v(nm) > 0:
                eng.wait_ge(self.s(nm), self.v(nm))


_NAME_N = [0]


def _sb(nc, ctx, name, shape, dt):
    _NAME_N[0] += 1
    return ctx.enter_context(nc.sbuf_tensor(f"{name}_{_NAME_N[0]}", shape, dt))


def _ps(nc, ctx, name, shape, dt=F32):
    _NAME_N[0] += 1
    return ctx.enter_context(nc.psum_tensor(f"{name}_{_NAME_N[0]}", shape, dt))



def emit_consts(nc, blk, e, onec, one32, onebr=None):
    def g(g):
        e.inc(g.memset(onec[:, :], 1.0), "init", 1)
        e.inc(g.memset(one32[:, :], 1.0), "init", 1)
        if onebr is not None:
            e.inc(g.memset(onebr[:, :], 1.0), "init", 1)
    blk.gpsimd(g)


def emit_ln(nc, blk, e, x_dram, xsb, buf, onec, one32, ps, s1row=None):
    sq, arow, brow, bc_a, bc_b, t32 = (buf["sq"], buf["arow"], buf["brow"],
                                       buf["bc_a"], buf["bc_b"], buf["t32"])

    def g_load(g):
        for c in range(NCD):
            e.inc(g.dma_start(xsb[:, c, :], x_dram[c]), "ld", 16)
    blk.gpsimd(g_load)
    ldv = e.v("ld")
    mm0 = e.v("lnmm")

    def s_sq(s):
        e.wait_now(s, "init")
        for c in range(NCD):
            s.wait_ge(e.s("ld"), ldv - (NCD - 1 - c) * 16)
            if c >= 2:
                s.wait_ge(e.s("lnmm"), mm0 + (c - 1) * NQB)
            e.inc(s.activation(sq[:, c % 2, :], xsb[:, c, :], AF.Square), "lnsq", 1)
    blk.scalar(s_sq)
    sqv = e.v("lnsq")

    def t_sums(t):
        e.wait_now(t, "init")
        for c in range(NCD):
            t.wait_ge(e.s("ld"), ldv - (NCD - 1 - c) * 16)
            t.wait_ge(e.s("lnsq"), sqv - (NCD - 1 - c))
            for qb in range(NQB):
                t.matmul(ps[qb][0:1, :], onec[:, :], xsb[:, c, qb * QB:(qb + 1) * QB],
                         start=(c == 0), stop=(c == NCD - 1))
                mi = t.matmul(ps[4 + qb][0:1, :], onec[:, :],
                              sq[:, c % 2, qb * QB:(qb + 1) * QB],
                              start=(c == 0), stop=(c == NCD - 1))
                e.inc(mi, "lnmm", 1)
    blk.tensor(t_sums)

    def v_stats(v):
        e.wait_now(v, "lnmm")
        for qb in range(NQB):
            sl = slice(qb * QB, (qb + 1) * QB)
            e.inc(v.tensor_scalar_mul(arow[:, sl], ps[qb][0:1, :], 1.0 / D), "tkv", 1)
            e.inc(v.tensor_scalar_mul(brow[:, sl], ps[4 + qb][0:1, :], 1.0 / D), "tkv", 1)
        e.wait_now(v, "tkv")
        e.inc(v.tensor_mul(t32[0:1, 0:T], arow[:, 0:T], arow[:, 0:T]), "tkv", 1)
        e.wait_now(v, "tkv")
        e.inc(v.tensor_sub(brow[:, 0:T], brow[:, 0:T], t32[0:1, 0:T]), "tkv", 1)
        e.wait_now(v, "tkv")
        e.inc(v.tensor_scalar_add(brow[:, 0:T], brow[:, 0:T], EPS), "lndve", 1)
    blk.vector(v_stats)

    def s_rstd(s):
        e.wait_now(s, "lndve")
        e.inc(s.sqrt(brow[:, 0:T], brow[:, 0:T]), "lnsq", 1)
    blk.scalar(s_rstd)

    def v_fin(v):
        e.wait_now(v, "lnsq")
        e.inc(v.reciprocal(brow[:, 0:T], brow[:, 0:T]), "tkv", 1)
        e.wait_now(v, "tkv")
        e.inc(v.tensor_mul(t32[0:1, 0:T], arow[:, 0:T], brow[:, 0:T]), "tkv", 1)
        e.inc(v.tensor_copy(arow[:, 0:T], brow[:, 0:T]), "tkv", 1)
        e.wait_now(v, "tkv")
        e.inc(v.tensor_scalar_mul(brow[:, 0:T], t32[0:1, 0:T], -1.0), "lndve", 1)
    blk.vector(v_fin)

    def t_bc(t):
        e.wait_now(t, "lndve")
        for qb in range(NQB):
            sl = slice(qb * QB, (qb + 1) * QB)
            t.matmul(ps[qb][:, :], one32[:, :], arow[:, sl], start=True, stop=True)
            e.inc(t.matmul(ps[4 + qb][:, :], one32[:, :], brow[:, sl],
                           start=True, stop=True), "lnmm", 1)
    blk.tensor(t_bc)

    def v_bc(v):
        e.wait_now(v, "lnmm")
        for qb in range(NQB):
            sl = slice(qb * QB, (qb + 1) * QB)
            v.tensor_copy(bc_a[:, sl], ps[qb][:, :])
            e.inc(v.tensor_copy(bc_b[:, sl], ps[4 + qb][:, :]), "tkv", 1)
        e.wait_now(v, "tkv")
        for c in range(NCD):
            e.inc(v.tensor_mul(t32[:, 0:T], xsb[:, c, :], bc_a[:, 0:T]), "tkv", 1)
            e.wait_now(v, "tkv")
            e.inc(v.tensor_add(xsb[:, c, :], t32[:, 0:T], bc_b[:, 0:T]), "lndve", 1)
    blk.vector(v_bc)

    if s1row is not None:
        def t_s1(t):
            e.wait_now(t, "lndve")
            for qb in range(NQB):
                for c in range(NCD):
                    mi = t.matmul(ps[qb][0:1, :], onec[:, :],
                                  xsb[:, c, qb * QB:(qb + 1) * QB],
                                  start=(c == 0), stop=(c == NCD - 1))
                e.inc(mi, "lnmm", 1)
        blk.tensor(t_s1)

        def v_s1(v):
            e.wait_now(v, "lnmm")
            for qb in range(NQB):
                e.inc(v.tensor_copy(s1row[:, qb * QB:(qb + 1) * QB], ps[qb][0:1, :]),
                      "lndve", 1)
        blk.vector(v_s1)


def emit_rowmax_sq(nc, blk, e, src_ap, sqh, onec, ps4, mrow, dst_ap, gate):
    def s_sq(s):
        e.wait_now(s, gate, "mmm")
        e.inc(s.activation(sqh[:, :], src_ap, AF.Square), "msq", 1)
    blk.scalar(s_sq)

    def t_cs(t):
        e.wait_now(t, "msq", "mdve")
        for qb in range(NQB):
            e.inc(t.matmul(ps4[qb][0:1, :], onec[:, :], sqh[:, qb * QB:(qb + 1) * QB],
                           start=True, stop=True), "mmm", 1)
    blk.tensor(t_cs)

    def v_cp(v):
        e.wait_now(v, "mmm")
        for qb in range(NQB):
            e.inc(v.tensor_copy(mrow[:, qb * QB:(qb + 1) * QB], ps4[qb][0:1, :]), "tkv", 1)
        e.wait_now(v, "tkv")
        e.inc(v.reduce_max(dst_ap, mrow[:, 0:T], axis=AX), "mdve", 1)
    blk.vector(v_cp)



def build_ln_qkv0(fused=None):
    if fused is None:
        nc = bass.Bass("TRN2", target_bir_lowering=False)
        x_fm = nc.dram_tensor("x_fm", [NCD, 128, T], F32, kind="ExternalInput").ap()
        wq_t = nc.dram_tensor("wq_t", [NH, D, DH], F32, kind="ExternalInput").ap()
        wk_t = nc.dram_tensor("wk_t", [NH, D, DH], F32, kind="ExternalInput").ap()
        wv_t = nc.dram_tensor("wv_t", [NH, D, DH], F32, kind="ExternalInput").ap()
        xn0 = nc.dram_tensor("xn0", [NCD, 128, T], BF16, kind="ExternalOutput").ap()
        s1 = nc.dram_tensor("s1", [1, T], F32, kind="ExternalOutput").ap()
        q0 = nc.dram_tensor("q0", [NH, 128, T], BF16, kind="ExternalOutput").ap()
        k0 = nc.dram_tensor("k0", [NH, 128, T], BF16, kind="ExternalOutput").ap()
        v0 = nc.dram_tensor("v0", [NTT, NH // 4, 128, 512], BF16, kind="ExternalOutput").ap()
        qm2 = nc.dram_tensor("qm2", [1, NH], F32, kind="ExternalOutput").ap()
        km2 = nc.dram_tensor("km2", [1, NH], F32, kind="ExternalOutput").ap()
        kvsh = None
    else:
        nc, blk0, e0, io = fused
        x_fm, wq_t, wk_t, wv_t = io["x_fm"], io["wq_t"], io["wk_t"], io["wv_t"]
        xn0, s1, q0, qm2 = io["xn0"], io["s1"], io["q0"], io["qm2"]
        k0 = v0 = km2 = None
        kvsh = io["kvsh"]

    with ExitStack() as ctx:
        xsb = _sb(nc, ctx, "xsb", [128, NCD, T], BF16)
        buf = dict(sq=_sb(nc, ctx, "sq", [128, 2, T], BF16),
                   arow=_sb(nc, ctx, "arow", [1, T], F32),
                   brow=_sb(nc, ctx, "brow", [1, T], F32),
                   bc_a=_sb(nc, ctx, "bc_a", [128, T], F32),
                   bc_b=_sb(nc, ctx, "bc_b", [128, T], F32),
                   t32=_sb(nc, ctx, "t32", [128, T], F32))
        s1row = _sb(nc, ctx, "s1row", [1, T], F32)
        onec = _sb(nc, ctx, "onec", [128, 1], BF16)
        one32 = _sb(nc, ctx, "one32", [1, 128], F32)
        wq_sb = _sb(nc, ctx, "wq_sb", [128, 2, NCD, DH], BF16)
        wk_sb = _sb(nc, ctx, "wk_sb", [128, 2, NCD, DH], BF16)
        wv_sb = _sb(nc, ctx, "wv_sb", [128, 2, NCD, 512], BF16)
        qh = _sb(nc, ctx, "qh", [128, T], BF16)
        kh = _sb(nc, ctx, "kh", [128, T], BF16)
        vo = _sb(nc, ctx, "vo", [128, NTT, 512], BF16)
        sqh = _sb(nc, ctx, "sqh", [128, T], BF16)
        mrow = _sb(nc, ctx, "mrow", [1, T], F32)
        qm2_sb = _sb(nc, ctx, "qm2_sb", [1, NH], F32)
        km2_sb = _sb(nc, ctx, "km2_sb", [1, NH], F32)
        ps = [_ps(nc, ctx, f"ps{i}", [128, 512]) for i in range(8)]
        if fused is None:
            e = E(nc, ctx)
            blk = ctx.enter_context(nc.Block())
        else:
            e, blk = e0, blk0

        emit_consts(nc, blk, e, onec, one32)
        emit_ln(nc, blk, e, x_fm, xsb, buf, onec, one32, ps, s1row=s1row)

        def g_store_xn(g):
            e.wait_now(g, "lndve")
            for c in range(NCD):
                e.inc(g.dma_start(xn0[c], xsb[:, c, :]), "st", 16)
            e.inc(g.dma_start(s1, s1row[:, :]), "st", 16)
        blk.gpsimd(g_store_xn)

        for h in range(NH):
            hb = h % 2
            vb = (h // 4) % 2

            def g_w(g, h=h, hb=hb, vb=vb):
                if h >= 2:
                    g.wait_ge(e.s("mm"), 8 * (h - 1))
                if h % 4 == 0 and h >= 8:
                    g.wait_ge(e.s("mmv"), 16 * (h // 4 - 1))
                wq_r = wq_t[h].rearrange("(c p) dh -> c p dh", p=128)
                wk_r = wk_t[h].rearrange("(c p) dh -> c p dh", p=128)
                for c in range(NCD):
                    e.inc(g.dma_start(wq_sb[:, hb, c, :], wq_r[c]), "wl", 16)
                    e.inc(g.dma_start(wk_sb[:, hb, c, :], wk_r[c]), "wl", 16)
                if h % 4 == 0:
                    for hs in range(4):
                        wv_r = wv_t[h + hs].rearrange("(c p) dh -> c p dh", p=128)
                        for c in range(NCD):
                            e.inc(g.dma_start(wv_sb[:, vb, c, hs * DH:(hs + 1) * DH],
                                              wv_r[c]), "wl", 16)
            blk.gpsimd(g_w)
            wlv_h = e.v("wl")

            def t_qk(t, h=h, hb=hb, wlv_h=wlv_h):
                t.wait_ge(e.s("wl"), wlv_h)
                e.wait_now(t, "lndve", "evq", "mdve")
                for qb in range(NQB):
                    for c in range(NCD):
                        mi = t.matmul(ps[qb][:, :], wq_sb[:, hb, c, :],
                                      xsb[:, c, qb * QB:(qb + 1) * QB],
                                      start=(c == 0), stop=(c == NCD - 1))
                    e.inc(mi, "mm", 1)
                for qb in range(NQB):
                    for c in range(NCD):
                        mi = t.matmul(ps[4 + qb][:, :], wk_sb[:, hb, c, :],
                                      xsb[:, c, qb * QB:(qb + 1) * QB],
                                      start=(c == 0), stop=(c == NCD - 1))
                    e.inc(mi, "mm", 1)
            blk.tensor(t_qk)

            def v_ev(v, h=h):
                e.wait_now(v, "mm", "stqk", "st")
                for qb in range(NQB):
                    v.tensor_copy(qh[:, qb * QB:(qb + 1) * QB], ps[qb][:, :])
                for qb in range(NQB):
                    mi = v.tensor_copy(kh[:, qb * QB:(qb + 1) * QB], ps[4 + qb][:, :])
                e.inc(mi, "evq", 1)
            blk.vector(v_ev)

            def g_st(g, h=h):
                e.wait_now(g, "evq")
                e.inc(g.dma_start(q0[h], qh[:, :]), "st", 16)
                if kvsh is None:
                    e.inc(g.dma_start(k0[h], kh[:, :]), "stqk", 16)
                else:
                    with g.If_eq(kvsh["rank_reg"], 0):
                        e.inc(g.dma_start(kvsh["kf"][0][h], kh[:, :]), "stqk", 16)
                    with g.Else():
                        g.dma_start(kvsh["kf"][1][h], kh[:, :]).then_inc(e.s("stqk"), 16)
            blk.gpsimd(g_st)

            emit_rowmax_sq(nc, blk, e, qh[:, :], sqh, onec, ps[0:4], mrow,
                           qm2_sb[0:1, h:h + 1], "evq")
            emit_rowmax_sq(nc, blk, e, kh[:, :], sqh, onec, ps[0:4], mrow,
                           km2_sb[0:1, h:h + 1], "evq")

            if h % 4 == 3:
                vev0 = e.v("vev")

                def t_v(t, vev0=vev0, vb=vb):
                    e.wait_now(t, "mdve")
                    for tt in range(NTT):
                        if tt >= 2:
                            t.wait_ge(e.s("vev"), vev0 + tt - 1)
                        for c in range(NCD):
                            mi = t.matmul(ps[tt % 2][:, :],
                                          xsb[:, c, tt * 128:(tt + 1) * 128],
                                          wv_sb[:, vb, c, :],
                                          start=(c == 0), stop=(c == NCD - 1))
                        e.inc(mi, "mmv", 1)
                blk.tensor(t_v)
                mmv = e.v("mmv")

                def v_vev(v, mmv=mmv):
                    for tt in range(NTT):
                        v.wait_ge(e.s("mmv"), mmv - NTT + tt + 1)
                        e.inc(v.tensor_copy(vo[:, tt, :], ps[tt % 2][:, :]), "vev", 1)
                blk.vector(v_vev)

                def g_stv(g, hg=h // 4):
                    e.wait_now(g, "vev")
                    if kvsh is None:
                        for tt in range(NTT):
                            e.inc(g.dma_start(v0[tt, hg], vo[:, tt, :]), "st", 16)
                    else:
                        with g.If_eq(kvsh["rank_reg"], 0):
                            for tt in range(NTT):
                                e.inc(g.dma_start(
                                    kvsh["vf"][0][tt][:, hg * 512:(hg + 1) * 512],
                                    vo[:, tt, :]), "st", 16)
                        with g.Else():
                            for tt in range(NTT):
                                g.dma_start(
                                    kvsh["vf"][1][tt][:, hg * 512:(hg + 1) * 512],
                                    vo[:, tt, :]).then_inc(e.s("st"), 16)
                blk.gpsimd(g_stv)

        def g_stm(g):
            e.wait_now(g, "mdve")
            e.inc(g.dma_start(qm2, qm2_sb[:, :]), "st", 16)
            if kvsh is None:
                e.inc(g.dma_start(km2, km2_sb[:, :]), "st", 16)
            else:
                with g.If_eq(kvsh["rank_reg"], 0):
                    e.inc(g.dma_start(kvsh["km2"][0:1, :], km2_sb[:, :]), "st", 16)
                with g.Else():
                    g.dma_start(kvsh["km2"][1:2, :], km2_sb[:, :]).then_inc(e.s("st"), 16)
            g.wait_ge(e.s("st"), e.v("st"))
            g.wait_ge(e.s("stqk"), e.v("stqk"))
            if kvsh is not None:
                with g.If_eq(kvsh["rank_reg"], 0):
                    e.inc(g.dma_start(kvsh["flags"][0:1, kvsh["fcol"]:kvsh["fcol"] + 1],
                                      kvsh["nonce_sb"][0:1, :]), "flg", 16)
                with g.Else():
                    g.dma_start(kvsh["flags"][1:2, kvsh["fcol"]:kvsh["fcol"] + 1],
                                kvsh["nonce_sb"][0:1, :]).then_inc(e.s("flg"), 16)
        blk.gpsimd(g_stm)
    return nc


def emit_poll(nc, blk, e, flags_ap, fcol, rank_reg, nonce_sb, fs_sb, tag):
    def g_poll(g):
        psem = e.s("poll_" + tag)
        with g.register("fv" + tag) as fv, g.register("cn" + tag) as cnt,              g.register("nv" + tag) as nv, g.register("ne" + tag) as ne:
            g.load(nv, nonce_sb[0:1, 0:1])
            g.reg_mov(fv, 0)
            g.reg_mov(cnt, 0)
            g.reg_alu(ne, fv, nv, ALU.not_equal)
            with g.While(ne):
                g.reg_add(cnt, cnt, 16)
                with g.If_eq(rank_reg, 0):
                    g.dma_start(fs_sb[0:1, :], flags_ap[1:2, fcol:fcol + 1]).then_inc(psem, 16)
                with g.Else():
                    g.dma_start(fs_sb[0:1, :], flags_ap[0:1, fcol:fcol + 1]).then_inc(psem, 16)
                g.wait_ge(psem, cnt)
                g.load(fv, fs_sb[0:1, 0:1])
                g.reg_alu(ne, fv, nv, ALU.not_equal)
    blk.gpsimd(g_poll)


def phase_barrier(blk, e):
    names = [n for n in list(e.cnt.keys()) if e.cnt[n] > 0 and not n.startswith("poll_")]
    b = e.s("bar")
    tgt = e.cnt["bar"] + 5
    for en in ("tensor", "scalar", "vector", "gpsimd", "sync"):
        def f(eng, tgt=tgt, names=names):
            for nm in names:
                eng.wait_ge(e.s(nm), e.cnt[nm])
            eng.nop().then_inc(b, 1)
            eng.wait_ge(b, tgt)
        getattr(blk, en)(f)
    e.cnt["bar"] = tgt



def build_attn(layer, fused=None):
    causal = (layer == 0)
    nkv = NH if causal else NKV
    if fused is None:
        nc = bass.Bass("TRN2", target_bir_lowering=False)
        q_in = nc.dram_tensor("q_in", [NH, 128, T], BF16, kind="ExternalInput").ap()
        kf = nc.dram_tensor("kf", [nkv, 128, SK], BF16, kind="ExternalInput").ap()
        vf = nc.dram_tensor("vf", [NKT, 128, nkv * DH], BF16, kind="ExternalInput").ap()
        qm2 = nc.dram_tensor("qm2", [1, NH], F32, kind="ExternalInput").ap()
        km2 = nc.dram_tensor("km2", [2, nkv], F32, kind="ExternalInput").ap()
        x_res = nc.dram_tensor("x_res", [NCD, 128, T],
                               F32 if layer == 0 else BF16, kind="ExternalInput").ap()
        wo_in = nc.dram_tensor("wo_in", [NH, DH, D] if causal else [D, D], F32,
                               kind="ExternalInput").ap()
        if causal:
            delta = nc.dram_tensor("delta", [128, 128], F32, kind="ExternalInput").ap()
        else:
            bo = nc.dram_tensor("bo", [1, D], F32, kind="ExternalInput").ap()
        xa = nc.dram_tensor("xa", [NCD, 128, T], BF16, kind="ExternalOutput").ap()
        kvsh = None
    else:
        nc, blk0, e0, io = fused
        q_in, kf, vf, qm2, km2 = io["q_in"], io["kf"], io["vf"], io["qm2"], io["km2"]
        x_res, wo_in, xa = io["x_res"], io["wo_in"], io["xa"]
        delta = io.get("delta")
        bo = io.get("bo")
        kvsh = io["kvsh"]

    with ExitStack() as ctx:
        onec = _sb(nc, ctx, "onec", [128, 1], BF16)
        one32 = _sb(nc, ctx, "one32", [1, 128], F32)
        onebr = _sb(nc, ctx, "onebr", [1, QB], BF16)
        qsb = _sb(nc, ctx, "qsb", [128, T], BF16)
        ksb = _sb(nc, ctx, "ksb", [128, SK], BF16)
        vsb = _sb(nc, ctx, "vsb", [128, NKT, DH], BF16)
        zt = _sb(nc, ctx, "zt", [128, NH, T], BF16)
        pt = _sb(nc, ctx, "pt", [128, NKT, QB], BF16)
        rrow = _sb(nc, ctx, "rrow", [1, QB], F32)
        rbc = _sb(nc, ctx, "rbc", [128, QB], F32)
        brow = _sb(nc, ctx, "brow", [1, NH], F32)
        bcol = _sb(nc, ctx, "bcol", [128, 1], F32)
        m2 = _sb(nc, ctx, "m2", [1, NH + 2 * NKV + 2 * NH], F32)
        idx = _sb(nc, ctx, "idx", [128, QB], F32) if causal else None
        dlt = _sb(nc, ctx, "dlt", [128, 128], F32) if causal else None
        msk = _sb(nc, ctx, "msk", [128, QB], BF16) if causal else None
        wo_sb = _sb(nc, ctx, "wo_sb", [128, NH, D], BF16)
        bo_sb = _sb(nc, ctx, "bo_sb", [1, D], BF16) if not causal else None
        xt = _sb(nc, ctx, "xt", [128, 2, QB], F32 if layer == 0 else BF16)
        ot = _sb(nc, ctx, "ot", [128, 2, QB], BF16)
        ps_s = [_ps(nc, ctx, f"ps_s{i}", [128, 512]) for i in range(2)]
        ps_z = [_ps(nc, ctx, f"ps_z{i}", [128, 512]) for i in range(2)]
        ps_r = _ps(nc, ctx, "ps_r", [128, 512])
        ps_b = _ps(nc, ctx, "ps_b", [128, 512])
        ps_o = [_ps(nc, ctx, f"ps_o{i}", [128, 512]) for i in range(2)]
        if fused is None:
            e = E(nc, ctx)
            blk = ctx.enter_context(nc.Block())
        else:
            e, blk = e0, blk0

        emit_consts(nc, blk, e, onec, one32, onebr)
        if kvsh is not None:
            emit_poll(nc, blk, e, kvsh["flags"], kvsh["fcol"], kvsh["rank_reg"],
                      kvsh["nonce_sb"], kvsh["fs_sb"], f"at{layer}")

        def g_pre(g):
            if causal:
                e.inc(g.iota(idx[:, :], pattern=[[256, 4], [1, 128]],
                             channel_multiplier=-1,
                             allow_small_or_imprecise_dtypes=True), "init", 1)
                e.inc(g.dma_start(dlt[:, :], delta), "ld", 16)
            e.inc(g.dma_start(m2[0:1, 0:NH], qm2), "ld", 16)
            e.inc(g.dma_start(m2[0:1, NH:NH + nkv], km2[0:1, :]), "ld", 16)
            e.inc(g.dma_start(m2[0:1, NH + nkv:NH + 2 * nkv], km2[1:2, :]), "ld", 16)
            if causal:
                for h in range(NH):
                    e.inc(g.dma_start(wo_sb[:, h, :], wo_in[h]), "wol", 16)
            else:
                wr = wo_in.rearrange("(c p) d2 -> c p d2", p=128)
                for c in range(NCD):
                    e.inc(g.dma_start(wo_sb[:, c, :], wr[c]), "wol", 16)
                e.inc(g.dma_start(bo_sb[:, :], bo), "wol", 16)
        blk.gpsimd(g_pre)

        def v_b(v):
            e.wait_now(v, "ld")
            e.inc(v.tensor_max(m2[0:1, NH:NH + nkv], m2[0:1, NH:NH + nkv],
                               m2[0:1, NH + nkv:NH + 2 * nkv]), "tkv", 1)
            e.wait_now(v, "tkv")
            if causal:
                e.inc(v.tensor_mul(brow[:, 0:NH], m2[0:1, 0:NH],
                                   m2[0:1, NH:NH + NH]), "bd", 1)
            else:
                for h in range(NH):
                    mi = v.tensor_mul(brow[:, h:h + 1], m2[0:1, h:h + 1],
                                      m2[0:1, NH + h // 4:NH + h // 4 + 1])
                e.inc(mi, "bd", 1)
        blk.vector(v_b)

        def s_b(s):
            e.wait_now(s, "bd")
            e.inc(s.sqrt(brow[:, 0:NH], brow[:, 0:NH]), "bd", 1)
        blk.scalar(s_b)

        def v_b2(v):
            e.wait_now(v, "bd")
            e.inc(v.tensor_scalar_mul(brow[:, 0:NH], brow[:, 0:NH], -SCALE), "bd", 1)
        blk.vector(v_b2)

        att_mmz0 = e.v("mmz")
        att_znd0 = e.v("znd")
        att_rr0 = e.v("rr")
        for h in range(NH):
            g_kv = h if causal else h // 4
            load_kv = causal or (h % 4 == 0)

            def g_qkv(g, h=h, g_kv=g_kv, load_kv=load_kv):
                e.wait_now(g, "mms", "mmz")
                e.inc(g.dma_start(qsb[:, :], q_in[h]), "ld", 16)
                if load_kv:
                    if kvsh is None:
                        e.inc(g.dma_start(ksb[:, :], kf[g_kv]), "ld", 16)
                        for at in range(NKT):
                            e.inc(g.dma_start(vsb[:, at, :],
                                              vf[at][:, g_kv * DH:(g_kv + 1) * DH]),
                                  "ld", 16)
                    else:
                        e.inc(g.dma_start(ksb[:, 0:SK // 2], kf[0][g_kv]), "ld", 16)
                        e.inc(g.dma_start(ksb[:, SK // 2:SK], kf[1][g_kv]), "ld", 16)
                        for lt in range(NTT):
                            e.inc(g.dma_start(vsb[:, lt, :],
                                              vf[0][lt][:, g_kv * DH:(g_kv + 1) * DH]),
                                  "ld", 16)
                            e.inc(g.dma_start(vsb[:, NTT + lt, :],
                                              vf[1][lt][:, g_kv * DH:(g_kv + 1) * DH]),
                                  "ld", 16)
            blk.gpsimd(g_qkv)
            ldv = e.v("ld")

            def t_bcast(t, h=h):
                e.wait_now(t, "bd", "bcold", "znd")
                e.inc(t.matmul(ps_b[:, 0:1], one32[:, :], brow[:, h:h + 1],
                               start=True, stop=True), "mmb", 1)
            blk.tensor(t_bcast)

            def v_bcast(v):
                e.wait_now(v, "mmb")
                e.inc(v.tensor_copy(bcol[:, :], ps_b[:, 0:1]), "bcol", 1)
            blk.vector(v_bcast)

            for qb in range(NQB):
                nkt = (8 * qb + 8) if causal else NKT
                mstart = 8 * qb if causal else nkt + 1
                pb = qb % 2
                xp0 = e.v("xp")
                pm0 = e.v("pmk")

                def t_sc(t, h=h, qb=qb, nkt=nkt, ldv=ldv, xp0=xp0):
                    t.wait_ge(e.s("ld"), ldv)
                    e.wait_now(t, "bcol")
                    for kt in range(nkt):
                        if kt >= 2:
                            t.wait_ge(e.s("xp"), xp0 + kt - 1)
                        ko = ((kt & 1) * (SK // 2) + (kt >> 1) * 128) if kvsh is not None \
                            else kt * 128
                        e.inc(t.matmul(ps_s[kt % 2][:, :],
                                       ksb[:, ko:ko + 128],
                                       qsb[:, qb * QB:(qb + 1) * QB],
                                       start=True, stop=True), "mms", 1)
                blk.tensor(t_sc)
                ms0 = e.v("mms") - nkt

                def s_exp(s, nkt=nkt, ms0=ms0, h=h, qb=qb):
                    i = h * NQB + qb
                    if i >= 1:
                        s.wait_ge(e.s("mmz"), att_mmz0 + i)
                    for kt in range(nkt):
                        s.wait_ge(e.s("mms"), ms0 + kt + 1)
                        e.inc(s.activation(pt[:, kt, :], ps_s[kt % 2][:, :],
                                           AF.Exp, bias=bcol[:, :], scale=SCALE),
                              "xp", 1)
                blk.scalar(s_exp)
                xpv = e.v("xp")

                if causal:
                    def v_mask(v, qb=qb, nkt=nkt, mstart=mstart, xp0=xp0):
                        e.wait_now(v, "init")
                        for kt in range(mstart, nkt):
                            v.wait_ge(e.s("xp"), xp0 + kt + 1)
                            e.inc(v.tensor_scalar(msk[:, :], idx[:, :],
                                                  dlt[:, (qb * 32 + kt):(qb * 32 + kt) + 1],
                                                  None, ALU.is_gt), "tkv", 1)
                            e.wait_now(v, "tkv")
                            e.inc(v.tensor_mul(pt[:, kt, :], pt[:, kt, :], msk[:, :]),
                                  "pmk", 1)
                    blk.vector(v_mask)
                pmv = e.v("pmk")

                def t_zr(t, h=h, qb=qb, nkt=nkt, pb=pb, xpv=xpv, pmv=pmv):
                    i = h * NQB + qb
                    t.wait_ge(e.s("xp"), xpv)
                    if pmv:
                        t.wait_ge(e.s("pmk"), pmv)
                    if i >= 2:
                        t.wait_ge(e.s("znd"), att_znd0 + i - 1)
                    if i >= 1:
                        t.wait_ge(e.s("rr"), att_rr0 + i)
                    for kt in range(nkt):
                        vi = ((kt & 1) * NTT + (kt >> 1)) if kvsh is not None else kt
                        t.matmul(ps_z[pb][:, :], vsb[:, vi, :], pt[:, kt, :],
                                 start=(kt == 0), stop=(kt == nkt - 1))
                        mi = t.matmul(ps_r[0:1, :], onec[:, :], pt[:, kt, :],
                                      start=(kt == 0), stop=(kt == nkt - 1))
                    e.inc(mi, "mmz", 1)
                blk.tensor(t_zr)

                def v_zn(v, h=h, qb=qb, pb=pb):
                    i = h * NQB + qb
                    v.wait_ge(e.s("mmz"), att_mmz0 + i + 1)
                    e.inc(v.reciprocal(rrow[0:1, :], ps_r[0:1, 0:QB]), "rr", 1)
                blk.vector(v_zn)

                def t_rbc(t, h=h, qb=qb):
                    i = h * NQB + qb
                    e.wait_now(t, "rr")
                    if i >= 1:
                        t.wait_ge(e.s("znd"), att_znd0 + i)
                    e.inc(t.matmul(ps_b[:, :], one32[:, :], rrow[0:1, :],
                                   start=True, stop=True), "mmr", 1)
                blk.tensor(t_rbc)

                def v_zn2(v, h=h, qb=qb, pb=pb):
                    e.wait_now(v, "mmr")
                    e.inc(v.tensor_copy(rbc[:, :], ps_b[:, :]), "tkv", 1)
                    e.wait_now(v, "tkv")
                    e.inc(v.tensor_mul(zt[:, h, qb * QB:(qb + 1) * QB],
                                       ps_z[pb][:, :], rbc[:, :]), "znd", 1)
                blk.vector(v_zn2)

            def t_bfree(t):
                e.inc(t.nop(), "bcold", 1)
            blk.tensor(t_bfree)

        odve0 = e.v("odve")
        znd_all = e.v("znd")
        for qb in range(NQB):
            for dc in range(NCD):
                j = qb * NCD + dc
                pb = j % 2

                def g_x(g, dc=dc, qb=qb, j=j, pb=pb):
                    if j >= 2:
                        g.wait_ge(e.s("odve"), odve0 + 16 * (j - 1))
                    e.inc(g.dma_start(xt[:, pb, :],
                                      x_res[dc][:, qb * QB:(qb + 1) * QB]), "lx", 16)
                blk.gpsimd(g_x)
                lxv = e.v("lx")

                def t_o(t, dc=dc, qb=qb, pb=pb, j=j):
                    t.wait_ge(e.s("znd"), znd_all)
                    e.wait_now(t, "wol")
                    if j >= 2:
                        t.wait_ge(e.s("odve"), odve0 + 16 * (j - 1))
                    for h in range(NH):
                        mi = t.matmul(ps_o[pb][:, :],
                                      wo_sb[:, h, dc * 128:(dc + 1) * 128],
                                      zt[:, h, qb * QB:(qb + 1) * QB],
                                      start=(h == 0),
                                      stop=(causal and h == NH - 1))
                    if not causal:
                        mi = t.matmul(ps_o[pb][:, :],
                                      bo_sb[:, dc * 128:(dc + 1) * 128],
                                      onebr[:, :], start=False, stop=True)
                    e.inc(mi, "mmo", 1)
                blk.tensor(t_o)

                def v_o(v, pb=pb, lxv=lxv, j=j):
                    e.wait_now(v, "mmo")
                    if j >= 2:
                        v.wait_ge(e.s("odve"), odve0 + 16 * (j - 1))
                    v.wait_ge(e.s("lx"), lxv)
                    e.inc(v.tensor_add(ot[:, pb, :], ps_o[pb][:, :], xt[:, pb, :]),
                          "ov", 1)
                blk.vector(v_o)

                def g_so(g, dc=dc, qb=qb, pb=pb):
                    e.wait_now(g, "ov")
                    e.inc(g.dma_start(xa[dc][:, qb * QB:(qb + 1) * QB], ot[:, pb, :]),
                          "odve", 16)
                blk.gpsimd(g_so)

        def g_fin(g):
            g.wait_ge(e.s("odve"), e.v("odve"))
        blk.gpsimd(g_fin)
    return nc



def build_ffn0(fused=None):
    if fused is None:
        nc = bass.Bass("TRN2", target_bir_lowering=False)
        xn_in = nc.dram_tensor("xn_in", [NCD, 128, T], BF16, kind="ExternalInput").ap()
        s1 = nc.dram_tensor("s1", [1, T], F32, kind="ExternalInput").ap()
        g_wt = nc.dram_tensor("g_wt", [D, IF], I32, kind="ExternalInput").ap()
        u_wt = nc.dram_tensor("u_wt", [D, IF], I32, kind="ExternalInput").ap()
        d_wt = nc.dram_tensor("d_wt", [IF, D], I32, kind="ExternalInput").ap()
        gsc = nc.dram_tensor("gsc", [128, NIC], F32, kind="ExternalInput").ap()
        usc = nc.dram_tensor("usc", [128, NIC], F32, kind="ExternalInput").ap()
        dsc = nc.dram_tensor("dsc", [128, NCD], F32, kind="ExternalInput").ap()
        xa = nc.dram_tensor("xa", [NCD, 128, T], BF16, kind="ExternalInput").ap()
        x1 = nc.dram_tensor("x1", [NCD, 128, T], BF16, kind="ExternalOutput").ap()
        h_dram = nc.dram_tensor("h_dram", [NIC, 128, T], BF16, kind="Internal").ap()
        y_dram = nc.dram_tensor("y_dram", [NCD, 128, T], F32, kind="Internal").ap()
    else:
        nc, blk0, e0, io = fused
        xn_in, s1, xa, x1 = io["xn_in"], io["s1"], io["xa"], io["x1"]
        g_wt, u_wt, d_wt = io["g_wt"], io["u_wt"], io["d_wt"]
        gsc, usc, dsc = io["gsc"], io["usc"], io["dsc"]
        h_dram, y_dram = io["h_dram"], io["y_dram"]

    g_r = g_wt.rearrange("(dc p) (nic i) -> nic p dc i", p=128, i=128)
    u_r = u_wt.rearrange("(dc p) (nic i) -> nic p dc i", p=128, i=128)
    d_r = d_wt.rearrange("(nic p) (dc dd) -> nic p dc dd", p=128, dd=128)

    with ExitStack() as ctx:
        xnsb = _sb(nc, ctx, "xnsb", [128, NCD, T], BF16)
        wg_sb = _sb(nc, ctx, "wg_sb", [128, 2, NCD, 128], BF16)
        wu_sb = _sb(nc, ctx, "wu_sb", [128, 2, NCD, 128], BF16)
        wd_sb = _sb(nc, ctx, "wd_sb", [128, NCD, NCD, 128], BF16)
        gsc_sb = _sb(nc, ctx, "gsc_sb", [128, NIC], F32)
        usc_sb = _sb(nc, ctx, "usc_sb", [128, NIC], F32)
        dsc_sb = _sb(nc, ctx, "dsc_sb", [128, NCD], F32)
        s1row = _sb(nc, ctx, "s1row", [1, T], F32)
        s1bf = _sb(nc, ctx, "s1bf", [1, T], BF16)
        n128r = _sb(nc, ctx, "n128r", [1, 128], BF16)
        s2row = _sb(nc, ctx, "s2row", [1, T], F32)
        bcg = _sb(nc, ctx, "bcg", [128, T], F32)
        gsb = _sb(nc, ctx, "gsb", [128, QB], BF16)
        usb = _sb(nc, ctx, "usb", [128, QB], BF16)
        hsb = _sb(nc, ctx, "hsb", [128, 2, T], BF16)
        yt = _sb(nc, ctx, "yt", [128, 2, QB], F32)
        xat = _sb(nc, ctx, "xat", [128, QB], BF16)
        x1t = _sb(nc, ctx, "x1t", [128, QB], BF16)
        onec = _sb(nc, ctx, "onec", [128, 1], BF16)
        one32 = _sb(nc, ctx, "one32", [1, 128], F32)
        ps_g = [_ps(nc, ctx, f"ps_g{i}", [128, 512]) for i in range(2)]
        ps_u = [_ps(nc, ctx, f"ps_u{i}", [128, 512]) for i in range(2)]
        ps_s2 = [_ps(nc, ctx, f"ps_s2{i}", [128, 512]) for i in range(4)]
        if fused is None:
            e = E(nc, ctx)
            blk = ctx.enter_context(nc.Block())
        else:
            e, blk = e0, blk0

        emit_consts(nc, blk, e, onec, one32)

        def g_pre(g):
            for c in range(NCD):
                e.inc(g.dma_start(xnsb[:, c, :], xn_in[c]), "ld", 16)
            e.inc(g.dma_start(s1row[:, :], s1), "ld", 16)
            e.inc(g.dma_start(gsc_sb[:, :], gsc), "ld", 16)
            e.inc(g.dma_start(usc_sb[:, :], usc), "ld", 16)
            e.inc(g.dma_start(dsc_sb[:, :], dsc), "ld", 16)
        blk.gpsimd(g_pre)

        def g_n128(g):
            e.inc(g.memset(n128r[:, :], -128.0), "init", 1)
        blk.gpsimd(g_n128)

        def v_s1(v):
            e.wait_now(v, "ld")
            e.inc(v.tensor_copy(s1bf[:, :], s1row[:, :]), "s1d", 1)
        blk.vector(v_s1)

        for ic in range(NIC):
            b = ic % 2

            def g_w(g, ic=ic, b=b):
                if ic >= 2:
                    g.wait_ge(e.s("mmgu"), 4 * (ic - 1))
                for dc4 in range(4):
                    sl = slice(dc4 * 4, (dc4 + 1) * 4)
                    e.inc(g.dma_start(wg_sb[:, b, sl, :], g_r[ic][:, sl, :]), "wl", 16)
                    e.inc(g.dma_start(wu_sb[:, b, sl, :], u_r[ic][:, sl, :]), "wl", 16)
            blk.gpsimd(g_w)
            wlv = e.v("wl")

            def t_gu(t, ic=ic, b=b, wlv=wlv):
                t.wait_ge(e.s("wl"), wlv)
                e.wait_now(t, "s1d", "ld", "init")
                for qb in range(NQB):
                    if 4 * ic + qb - 1 >= 1:
                        t.wait_ge(e.s("hact"), 4 * ic + qb - 1)
                        t.wait_ge(e.s("hmul"), 4 * ic + qb - 1)
                    for c in range(NCD):
                        t.matmul(ps_g[qb % 2][:, :], wg_sb[:, b, c, :],
                                 xnsb[:, c, qb * QB:(qb + 1) * QB],
                                 start=(c == 0), stop=False)
                        t.matmul(ps_u[qb % 2][:, :], wu_sb[:, b, c, :],
                                 xnsb[:, c, qb * QB:(qb + 1) * QB],
                                 start=(c == 0), stop=False)
                    t.matmul(ps_g[qb % 2][:, :], n128r[:, :],
                             s1bf[:, qb * QB:(qb + 1) * QB], start=False, stop=True)
                    mi = t.matmul(ps_u[qb % 2][:, :], n128r[:, :],
                                  s1bf[:, qb * QB:(qb + 1) * QB], start=False, stop=True)
                    e.inc(mi, "mmgu", 1)
            blk.tensor(t_gu)
            mguv = e.v("mmgu")

            def s_act(s, ic=ic, mguv=mguv):
                for qb in range(NQB):
                    s.wait_ge(e.s("mmgu"), mguv - NQB + qb + 1)
                    if qb >= 1:
                        s.wait_ge(e.s("hmul"), 4 * ic + qb)
                    e.inc(s.activation(gsb[:, :], ps_g[qb % 2][:, :], AF.Silu,
                                       scale=gsc_sb[:, ic:ic + 1]), "hact", 1)
            blk.scalar(s_act)
            hav = e.v("hact")

            def v_h(v, ic=ic, b=b, hav=hav, mguv=mguv):
                for qb in range(NQB):
                    v.wait_ge(e.s("mmgu"), mguv - NQB + qb + 1)
                    e.inc(v.tensor_scalar_mul(usb[:, :], ps_u[qb % 2][:, :],
                                              usc_sb[:, ic:ic + 1]), "tkv", 1)
                    v.wait_ge(e.s("hact"), hav - NQB + qb + 1)
                    e.wait_now(v, "tkv")
                    if ic >= 2:
                        v.wait_ge(e.s("hst"), 16 * (ic - 1))
                    mi = v.tensor_mul(hsb[:, b, qb * QB:(qb + 1) * QB], gsb[:, :],
                                      usb[:, :])
                    e.inc(mi, "hmul", 1)
                e.inc(v.tensor_copy(hsb[:, b, 0:1], hsb[:, b, 0:1]), "hda", 1)
            blk.vector(v_h)
            hmv = e.v("hmul")
            hdav = e.v("hda")

            def t_s2(t, ic=ic, b=b, hdav=hdav):
                t.wait_ge(e.s("hda"), hdav)
                for qb in range(NQB):
                    e.inc(t.matmul(ps_s2[qb][0:1, :], onec[:, :],
                                   hsb[:, b, qb * QB:(qb + 1) * QB],
                                   start=(ic == 0), stop=(ic == NIC - 1)), "mms2", 1)
            blk.tensor(t_s2)

            def g_h(g, ic=ic, b=b, hdav=hdav):
                g.wait_ge(e.s("hda"), hdav)
                e.wait_now(g, "mms2")
                ii = g.dma_start(h_dram[ic], hsb[:, b, :])
                e.inc(ii, "hst", 16)
            blk.gpsimd(g_h)

        def v_s2(v):
            e.wait_now(v, "mms2")
            for qb in range(NQB):
                e.inc(v.tensor_copy(s2row[:, qb * QB:(qb + 1) * QB], ps_s2[qb][0:1, :]),
                      "tkv", 1)
            e.wait_now(v, "tkv")
            e.inc(v.tensor_scalar_mul(s2row[:, :], s2row[:, :], -128.0), "s2d", 1)
        blk.vector(v_s2)

        def t_bc2(t):
            e.wait_now(t, "s2d", "hst")
            banks = [ps_s2[0], ps_s2[1], ps_s2[2], ps_s2[3]]
            for qb in range(NQB):
                e.inc(t.matmul(banks[qb][:, :], one32[:, :],
                               s2row[:, qb * QB:(qb + 1) * QB], start=True, stop=True),
                      "mmbc2", 1)
        blk.tensor(t_bc2)

        def v_bc2(v):
            e.wait_now(v, "mmbc2")
            for qb in range(NQB):
                e.inc(v.tensor_copy(bcg[:, qb * QB:(qb + 1) * QB], ps_s2[qb][:, :]),
                      "bc2d", 1)
        blk.vector(v_bc2)

        ydve0 = e.v("ydve")
        yst0 = e.v("yst")
        dit = [0]
        for sg in range(4):
            def g_wd(g, sg=sg):
                if sg >= 1:
                    e.wait_now(g, "mmd")
                for icl in range(NCD):
                    for dc4 in range(4):
                        sl = slice(dc4 * 4, (dc4 + 1) * 4)
                        e.inc(g.dma_start(wd_sb[:, icl, sl, :],
                                          d_r[sg * 16 + icl][:, sl, :]), "wdl", 16)
                e.wait_now(g, "mmgu")
                for icl in range(NCD):
                    e.inc(g.dma_start(xnsb[:, icl, :], h_dram[sg * 16 + icl]),
                          "hl", 16)
            blk.gpsimd(g_wd)
            wdv = e.v("wdl")
            hlv = e.v("hl")

            for qb in range(NQB):
                for dc in range(NCD):
                    pb = (qb * NCD + dc) % 2

                    j = dit[0]
                    dit[0] += 1

                    def t_d(t, sg=sg, qb=qb, dc=dc, pb=pb, wdv=wdv, hlv=hlv, j=j):
                        t.wait_ge(e.s("wdl"), wdv)
                        t.wait_ge(e.s("hl"), hlv)
                        e.wait_now(t, "bc2d")
                        if j >= 2:
                            t.wait_ge(e.s("ydve"), ydve0 + j - 1)
                        for icl in range(NCD):
                            mi = t.matmul(ps_g[pb][:, :], wd_sb[:, icl, dc, :],
                                          xnsb[:, icl, qb * QB:(qb + 1) * QB],
                                          start=(icl == 0), stop=(icl == NCD - 1))
                        e.inc(mi, "mmd", 1)
                    blk.tensor(t_d)

                    def v_y(v, sg=sg, qb=qb, dc=dc, pb=pb, j=j):
                        e.wait_now(v, "mmd")
                        if j >= 2:
                            v.wait_ge(e.s("yst"), yst0 + 16 * (j - 1))
                        if sg == 0:
                            mi = v.tensor_add(yt[:, pb, :], ps_g[pb][:, :],
                                              bcg[:, qb * QB:(qb + 1) * QB])
                        else:
                            mi = v.tensor_copy(yt[:, pb, :], ps_g[pb][:, :])
                        e.inc(mi, "ydve", 1)
                    blk.vector(v_y)

                    def g_y(g, sg=sg, qb=qb, dc=dc, pb=pb):
                        e.wait_now(g, "ydve")
                        if sg == 0:
                            ii = g.dma_start(y_dram[dc][:, qb * QB:(qb + 1) * QB],
                                             yt[:, pb, :])
                        else:
                            ii = g.dma_start(y_dram[dc][:, qb * QB:(qb + 1) * QB],
                                             yt[:, pb, :], accum_op=ALU.add)
                        e.inc(ii, "yst", 16)
                    blk.gpsimd(g_y)

        for dc in range(NCD):
            for qb in range(NQB):
                def g_l(g, dc=dc, qb=qb):
                    e.wait_now(g, "yst", "x1d")
                    e.inc(g.dma_start(yt[:, 0, :],
                                      y_dram[dc][:, qb * QB:(qb + 1) * QB]), "fl", 16)
                    e.inc(g.dma_start(xat[:, :], xa[dc][:, qb * QB:(qb + 1) * QB]),
                          "fl", 16)
                blk.gpsimd(g_l)
                flv = e.v("fl")

                def v_f(v, dc=dc, qb=qb, flv=flv):
                    v.wait_ge(e.s("fl"), flv)
                    e.inc(v.scalar_tensor_tensor(x1t[:, :], yt[:, 0, :],
                                                 dsc_sb[:, dc:dc + 1], xat[:, :],
                                                 ALU.mult, ALU.add), "x1d", 1)
                blk.vector(v_f)

                def g_s(g, dc=dc, qb=qb):
                    e.wait_now(g, "x1d")
                    e.inc(g.dma_start(x1[dc][:, qb * QB:(qb + 1) * QB], x1t[:, :]),
                          "st", 16)
                blk.gpsimd(g_s)

        def g_fin(g):
            g.wait_ge(e.s("st"), e.v("st"))
            g.wait_ge(e.s("x1d"), e.v("x1d"))
        blk.gpsimd(g_fin)
    return nc



QKV1_STAGE = "all"


def build_ln_qkv1(fused=None):
    if fused is None:
        nc = bass.Bass("TRN2", target_bir_lowering=False)
        x_in = nc.dram_tensor("x_in", [NCD, 128, T], BF16, kind="ExternalInput").ap()
        wq = nc.dram_tensor("wq", [D, D], F32, kind="ExternalInput").ap()
        bq = nc.dram_tensor("bq", [1, D], F32, kind="ExternalInput").ap()
        wk = nc.dram_tensor("wk", [D, 512], F32, kind="ExternalInput").ap()
        bk = nc.dram_tensor("bk", [1, 512], F32, kind="ExternalInput").ap()
        wv = nc.dram_tensor("wv", [D, 512], F32, kind="ExternalInput").ap()
        bv = nc.dram_tensor("bv", [1, 512], F32, kind="ExternalInput").ap()
        rw = nc.dram_tensor("rw", [D, 16], F32, kind="ExternalInput").ap()
        rb = nc.dram_tensor("rb", [1, 16], F32, kind="ExternalInput").ap()
        xn1 = nc.dram_tensor("xn1", [NCD, 128, T], BF16, kind="ExternalOutput").ap()
        q1 = nc.dram_tensor("q1", [NH, 128, T], BF16, kind="ExternalOutput").ap()
        k1 = nc.dram_tensor("k1", [NKV, 128, T], BF16, kind="ExternalOutput").ap()
        v1 = nc.dram_tensor("v1", [NTT, 128, 512], BF16, kind="ExternalOutput").ap()
        s_t = nc.dram_tensor("s_t", [NTT, 128, 16], F32, kind="ExternalOutput").ap()
        qm2 = nc.dram_tensor("qm2", [1, NH], F32, kind="ExternalOutput").ap()
        km2 = nc.dram_tensor("km2", [1, NKV], F32, kind="ExternalOutput").ap()
        kvsh = None
    else:
        nc, blk0, e0, io = fused
        x_in = io["x_in"]
        wq, bq, wk, bk, wv, bv = (io["wq"], io["bq"], io["wk"], io["bk"],
                                  io["wv"], io["bv"])
        rw, rb = io["rw"], io["rb"]
        xn1, q1, s_t, qm2 = io["xn1"], io["q1"], io["s_t"], io["qm2"]
        k1 = v1 = km2 = None
        kvsh = io["kvsh"]

    wq_r = wq.rearrange("(c p) o -> c p o", p=128)
    wk_r = wk.rearrange("(c p) o -> c p o", p=128)
    wv_r = wv.rearrange("(c p) o -> c p o", p=128)
    rw_r = rw.rearrange("(c p) o -> c p o", p=128)

    with ExitStack() as ctx:
        xsb = _sb(nc, ctx, "xsb", [128, NCD, T], BF16)
        buf = dict(sq=_sb(nc, ctx, "sq", [128, 2, T], BF16),
                   arow=_sb(nc, ctx, "arow", [1, T], F32),
                   brow=_sb(nc, ctx, "brow", [1, T], F32),
                   bc_a=_sb(nc, ctx, "bc_a", [128, T], F32),
                   bc_b=_sb(nc, ctx, "bc_b", [128, T], F32),
                   t32=_sb(nc, ctx, "t32", [128, T], F32))
        onec = _sb(nc, ctx, "onec", [128, 1], BF16)
        one32 = _sb(nc, ctx, "one32", [1, 128], F32)
        onebr = _sb(nc, ctx, "onebr", [1, QB], BF16)
        wh_sb = _sb(nc, ctx, "wh_sb", [128, 2, NCD, 128], BF16)
        wv_sb = _sb(nc, ctx, "wv_sb", [128, NCD, 512], BF16)
        rw_sb = _sb(nc, ctx, "rw_sb", [128, NCD, 16], BF16)
        bq_sb = _sb(nc, ctx, "bq_sb", [1, D], BF16)
        bk_sb = _sb(nc, ctx, "bk_sb", [1, 512], BF16)
        bv_sb = _sb(nc, ctx, "bv_sb", [1, 512], BF16)
        rb_sb = _sb(nc, ctx, "rb_sb", [1, 16], BF16)
        qh = _sb(nc, ctx, "qh", [128, T], BF16)
        kh = _sb(nc, ctx, "kh", [128, T], BF16)
        vo = _sb(nc, ctx, "vo", [128, NTT, 512], BF16)
        sqh = _sb(nc, ctx, "sqh", [128, T], BF16)
        mrow = _sb(nc, ctx, "mrow", [1, T], F32)
        qm2_sb = _sb(nc, ctx, "qm2_sb", [1, NH], F32)
        km2_sb = _sb(nc, ctx, "km2_sb", [1, NKV], F32)
        ex_sb = _sb(nc, ctx, "ex_sb", [128, 16], F32)
        mx_sb = [_sb(nc, ctx, "mx_sb0", [128, 1], F32),
                 _sb(nc, ctx, "mx_sb1", [128, 1], F32)]
        ssum = _sb(nc, ctx, "ssum", [128, 1], F32)
        ssb = _sb(nc, ctx, "ssb", [128, 16], F32)
        ps = [_ps(nc, ctx, f"ps{i}", [128, 512]) for i in range(8)]
        if fused is None:
            e = E(nc, ctx)
            blk = ctx.enter_context(nc.Block())
        else:
            e, blk = e0, blk0

        emit_consts(nc, blk, e, onec, one32, onebr)
        emit_ln(nc, blk, e, x_in, xsb, buf, onec, one32, ps, s1row=None)

        def g_store_xn(g):
            e.wait_now(g, "lndve")
            for c in range(NCD):
                e.inc(g.dma_start(xn1[c], xsb[:, c, :]), "st", 16)
            e.inc(g.dma_start(bq_sb[:, :], bq), "bl", 16)
            e.inc(g.dma_start(bk_sb[:, :], bk), "bl", 16)
            e.inc(g.dma_start(bv_sb[:, :], bv), "bl", 16)
            e.inc(g.dma_start(rb_sb[:, :], rb), "bl", 16)
        blk.gpsimd(g_store_xn)

        mm_qkv1_0 = e.v("mm")
        for h in range(NH if QKV1_STAGE in ("all", "q", "k", "v") else 0):
            hb = h % 2

            def g_w(g, h=h, hb=hb):
                if h >= 2:
                    g.wait_ge(e.s("mm"), mm_qkv1_0 + 4 * (h - 1))
                for c in range(NCD):
                    e.inc(g.dma_start(wh_sb[:, hb, c, :],
                                      wq_r[c][:, h * 128:(h + 1) * 128]), "wl", 16)
            blk.gpsimd(g_w)
            wlv_h = e.v("wl")

            def t_q(t, h=h, hb=hb, wlv_h=wlv_h):
                t.wait_ge(e.s("wl"), wlv_h)
                e.wait_now(t, "lndve", "bl", "evq", "mdve")
                for qb in range(NQB):
                    for c in range(NCD):
                        t.matmul(ps[qb][:, :], wh_sb[:, hb, c, :],
                                 xsb[:, c, qb * QB:(qb + 1) * QB],
                                 start=(c == 0), stop=False)
                    mi = t.matmul(ps[qb][:, :], bq_sb[:, h * 128:(h + 1) * 128],
                                  onebr[:, :], start=False, stop=True)
                    e.inc(mi, "mm", 1)
            blk.tensor(t_q)

            def v_ev(v, h=h):
                e.wait_now(v, "mm", "stq")
                for qb in range(NQB):
                    mi = v.tensor_copy(qh[:, qb * QB:(qb + 1) * QB], ps[qb][:, :])
                e.inc(mi, "evq", 1)
            blk.vector(v_ev)

            def g_st(g, h=h):
                e.wait_now(g, "evq")
                e.inc(g.dma_start(q1[h], qh[:, :]), "stq", 16)
            blk.gpsimd(g_st)

            emit_rowmax_sq(nc, blk, e, qh[:, :], sqh, onec, ps[4:8], mrow,
                           qm2_sb[0:1, h:h + 1], "evq")

        for kv in range(NKV if QKV1_STAGE in ("all", "k", "v") else 0):
            hb = (NH + kv) % 2

            def g_w(g, kv=kv, hb=hb):
                g.wait_ge(e.s("mm"), mm_qkv1_0 + 4 * (NH + kv - 1))
                for c in range(NCD):
                    e.inc(g.dma_start(wh_sb[:, hb, c, :],
                                      wk_r[c][:, kv * 128:(kv + 1) * 128]), "wl", 16)
            blk.gpsimd(g_w)
            wlv_h = e.v("wl")

            def t_k(t, kv=kv, hb=hb, wlv_h=wlv_h):
                t.wait_ge(e.s("wl"), wlv_h)
                e.wait_now(t, "evq", "mdve")
                for qb in range(NQB):
                    for c in range(NCD):
                        t.matmul(ps[qb][:, :], wh_sb[:, hb, c, :],
                                 xsb[:, c, qb * QB:(qb + 1) * QB],
                                 start=(c == 0), stop=False)
                    mi = t.matmul(ps[qb][:, :], bk_sb[:, kv * 128:(kv + 1) * 128],
                                  onebr[:, :], start=False, stop=True)
                    e.inc(mi, "mm", 1)
            blk.tensor(t_k)

            def v_ev(v, kv=kv):
                e.wait_now(v, "mm", "stq")
                for qb in range(NQB):
                    mi = v.tensor_copy(kh[:, qb * QB:(qb + 1) * QB], ps[qb][:, :])
                e.inc(mi, "evq", 1)
            blk.vector(v_ev)

            def g_st(g, kv=kv):
                e.wait_now(g, "evq")
                if kvsh is None:
                    e.inc(g.dma_start(k1[kv], kh[:, :]), "stq", 16)
                else:
                    with g.If_eq(kvsh["rank_reg"], 0):
                        e.inc(g.dma_start(kvsh["kf"][0][kv], kh[:, :]), "stq", 16)
                    with g.Else():
                        g.dma_start(kvsh["kf"][1][kv], kh[:, :]).then_inc(e.s("stq"), 16)
            blk.gpsimd(g_st)

            emit_rowmax_sq(nc, blk, e, kh[:, :], sqh, onec, ps[4:8], mrow,
                           km2_sb[0:1, kv:kv + 1], "evq")

        if QKV1_STAGE not in ("all", "v", "vn", "rn"):
            def g_end(g):
                g.wait_ge(e.s("st"), e.v("st"))
                g.wait_ge(e.s("stq"), e.v("stq"))
            blk.gpsimd(g_end)
            return nc

        def g_wv(g):
            e.wait_now(g, "mm")
            for c in range(NCD):
                e.inc(g.dma_start(wv_sb[:, c, :], wv_r[c]), "wl", 16)
                e.inc(g.dma_start(rw_sb[:, c, :], rw_r[c]), "wl", 16)
        blk.gpsimd(g_wv)
        wlv = e.v("wl")
        vev0 = e.v("vev")

        def t_v(t, wlv=wlv, vev0=vev0):
            t.wait_ge(e.s("wl"), wlv)
            e.wait_now(t, "mdve")
            sexp0 = e.v("sexp")
            for tt in range(NTT):
                if tt >= 2:
                    t.wait_ge(e.s("vev"), vev0 + tt - 1)
                    if QKV1_STAGE != "vn":
                        t.wait_ge(e.s("sexp"), sexp0 + tt - 1)
                if QKV1_STAGE != "rn":
                    for c in range(NCD):
                        t.matmul(ps[tt % 2][:, :],
                                 xsb[:, c, tt * 128:(tt + 1) * 128],
                                 wv_sb[:, c, :], start=(c == 0), stop=False)
                    mi = t.matmul(ps[tt % 2][:, :], onebr[:, 0:128], bv_sb[:, :],
                                  start=False, stop=True)
                else:
                    mi = t.matmul(ps[tt % 2][:, :], onebr[:, 0:128], bv_sb[:, :],
                                  start=True, stop=True)
                e.inc(mi, "mmv", 1)
                if QKV1_STAGE != "vn":
                    for c in range(NCD):
                        t.matmul(ps[2 + tt % 2][:, 0:16], xsb[:, c, tt * 128:(tt + 1) * 128],
                                 rw_sb[:, c, :], start=(c == 0), stop=False)
                    mi = t.matmul(ps[2 + tt % 2][:, 0:16], onebr[:, 0:128], rb_sb[:, :],
                                  start=False, stop=True)
                    e.inc(mi, "mmr", 1)
        blk.tensor(t_v)
        mmvv = e.v("mmv")
        mmrv = e.v("mmr")

        sexp0 = e.v("sexp")
        sdone0 = e.v("sdone")
        sst0 = e.v("sst")

        def v_vr(v, mmvv=mmvv, mmrv=mmrv, sexp0=sexp0, sdone0=sdone0, sst0=sst0):
            def finish(tt):
                v.wait_ge(e.s("sexp"), sexp0 + tt + 1)
                if tt >= 1:
                    v.wait_ge(e.s("sst"), sst0 + 16 * tt)
                e.inc(v.reduce_sum(ssum[:, :], ex_sb[:, :], axis=AX), "tkv", 1)
                e.wait_now(v, "tkv")
                e.inc(v.reciprocal(ssum[:, :], ssum[:, :]), "tkv", 1)
                e.wait_now(v, "tkv")
                e.inc(v.tensor_scalar_mul(ssb[:, :], ex_sb[:, :], ssum[:, :]),
                      "sdone", 1)

            for tt in range(NTT):
                v.wait_ge(e.s("mmv"), mmvv - NTT + tt + 1)
                e.inc(v.tensor_copy(vo[:, tt, :], ps[tt % 2][:, :]), "vev", 1)
                v.wait_ge(e.s("mmr"), mmrv - NTT + tt + 1)
                if tt >= 2:
                    v.wait_ge(e.s("sexp"), sexp0 + tt - 1)
                e.inc(v.reduce_max(mx_sb[tt % 2][:, :],
                                   ps[2 + tt % 2][:, 0:16], axis=AX,
                                   negate=True), "smax", 1)
                if tt >= 1:
                    finish(tt - 1)
            finish(NTT - 1)
        blk.vector(v_vr)
        smv = e.v("smax")
        vevv = e.v("vev")

        def s_exp(s, smv=smv, sdone0=sdone0):
            for tt in range(NTT if QKV1_STAGE != "vn" else 0):
                s.wait_ge(e.s("smax"), smv - NTT + tt + 1)
                if tt >= 1:
                    s.wait_ge(e.s("sdone"), sdone0 + tt)
                e.inc(s.activation(ex_sb[:, :], ps[2 + tt % 2][:, 0:16], AF.Exp,
                                   bias=mx_sb[tt % 2][:, :]), "sexp", 1)
        blk.scalar(s_exp)
        sev = e.v("sexp")
        sdv = e.v("sdone")

        def g_sv(g, vevv=vevv, sdv=sdv):
            for tt in range(NTT):
                g.wait_ge(e.s("vev"), vevv - NTT + tt + 1)
                if kvsh is None:
                    e.inc(g.dma_start(v1[tt], vo[:, tt, :]), "st", 16)
                else:
                    with g.If_eq(kvsh["rank_reg"], 0):
                        e.inc(g.dma_start(kvsh["vf"][0][tt], vo[:, tt, :]), "st", 16)
                    with g.Else():
                        g.dma_start(kvsh["vf"][1][tt], vo[:, tt, :]).then_inc(
                            e.s("st"), 16)
                if QKV1_STAGE == "vn":
                    continue
                g.wait_ge(e.s("sdone"), sdv - NTT + tt + 1)
                e.inc(g.dma_start(s_t[tt], ssb[:, :]), "sst", 16)
        blk.gpsimd(g_sv)

        def g_stm(g):
            e.wait_now(g, "mdve")
            e.inc(g.dma_start(qm2, qm2_sb[:, :]), "st", 16)
            if kvsh is None:
                e.inc(g.dma_start(km2, km2_sb[:, :]), "st", 16)
            else:
                with g.If_eq(kvsh["rank_reg"], 0):
                    e.inc(g.dma_start(kvsh["km2"][0:1, :], km2_sb[:, :]), "st", 16)
                with g.Else():
                    g.dma_start(kvsh["km2"][1:2, :], km2_sb[:, :]).then_inc(e.s("st"), 16)
            g.wait_ge(e.s("st"), e.v("st"))
            g.wait_ge(e.s("sst"), e.v("sst"))
            g.wait_ge(e.s("stq"), e.v("stq"))
            if kvsh is not None:
                with g.If_eq(kvsh["rank_reg"], 0):
                    e.inc(g.dma_start(kvsh["flags"][0:1, kvsh["fcol"]:kvsh["fcol"] + 1],
                                      kvsh["nonce_sb"][0:1, :]), "flg", 16)
                with g.Else():
                    g.dma_start(kvsh["flags"][1:2, kvsh["fcol"]:kvsh["fcol"] + 1],
                                kvsh["nonce_sb"][0:1, :]).then_inc(e.s("flg"), 16)
        blk.gpsimd(g_stm)
    return nc



def build_expert(fused=None):
    if fused is None:
        nc = bass.Bass("TRN2", target_bir_lowering=False)
        xn_in = nc.dram_tensor("xn_in", [NCD, 128, T], BF16, kind="ExternalInput").ap()
        ew = nc.dram_tensor("ew", [D, D], F32, kind="ExternalInput").ap()
        eb = nc.dram_tensor("eb", [1, D], F32, kind="ExternalInput").ap()
        y1 = nc.dram_tensor("y1", [NCD, 128, T], BF16, kind="ExternalOutput").ap()
    else:
        nc, blk0, e0, io = fused
        xn_in, ew, eb, y1 = io["xn_in"], io["ew"], io["eb"], io["y1"]
    ew_r = ew.rearrange("(c p) o -> c p o", p=128)

    with ExitStack() as ctx:
        xsb = _sb(nc, ctx, "xsb", [128, NCD, T], BF16)
        ew_sb = _sb(nc, ctx, "ew_sb", [128, NCD, D], BF16)
        eb_sb = _sb(nc, ctx, "eb_sb", [1, D], BF16)
        onebr = _sb(nc, ctx, "onebr", [1, QB], BF16)
        onec = _sb(nc, ctx, "onec", [128, 1], BF16)
        one32 = _sb(nc, ctx, "one32", [1, 128], F32)
        ot = _sb(nc, ctx, "ot", [128, 2, QB], BF16)
        ps = [_ps(nc, ctx, f"ps{i}", [128, 512]) for i in range(2)]
        if fused is None:
            e = E(nc, ctx)
            blk = ctx.enter_context(nc.Block())
        else:
            e, blk = e0, blk0

        emit_consts(nc, blk, e, onec, one32, onebr)

        def g_pre(g):
            for c in range(NCD):
                e.inc(g.dma_start(xsb[:, c, :], xn_in[c]), "ld", 16)
                e.inc(g.dma_start(ew_sb[:, c, :], ew_r[c]), "ld", 16)
            e.inc(g.dma_start(eb_sb[:, :], eb), "ld", 16)
        blk.gpsimd(g_pre)
        ldv = e.v("ld")

        for dc in range(NCD):
            for qb in range(NQB):
                pb = (dc * NQB + qb) % 2

                def t_e(t, dc=dc, qb=qb, pb=pb, ldv=ldv):
                    t.wait_ge(e.s("ld"), ldv)
                    e.wait_now(t, "init", "oev")
                    for c in range(NCD):
                        t.matmul(ps[pb][:, :], ew_sb[:, c, dc * 128:(dc + 1) * 128],
                                 xsb[:, c, qb * QB:(qb + 1) * QB],
                                 start=(c == 0), stop=False)
                    mi = t.matmul(ps[pb][:, :], eb_sb[:, dc * 128:(dc + 1) * 128],
                                  onebr[:, :], start=False, stop=True)
                    e.inc(mi, "mm", 1)
                blk.tensor(t_e)

                def v_e(v, pb=pb):
                    e.wait_now(v, "mm", "ost")
                    e.inc(v.tensor_copy(ot[:, pb, :], ps[pb][:, :]), "oev", 1)
                blk.vector(v_e)

                def g_o(g, dc=dc, qb=qb, pb=pb):
                    e.wait_now(g, "oev")
                    e.inc(g.dma_start(y1[dc][:, qb * QB:(qb + 1) * QB], ot[:, pb, :]),
                          "ost", 16)
                blk.gpsimd(g_o)

        def g_fin(g):
            g.wait_ge(e.s("ost"), e.v("ost"))
        blk.gpsimd(g_fin)
    return nc



def build_moe_final():
    NT_ALL = 16384
    nc = bass.Bass("TRN2", target_bir_lowering=False)
    s_all = nc.dram_tensor("s_all", [16, NT_ALL], F32, kind="ExternalInput").ap()
    s_own = nc.dram_tensor("s_own", [16, T], F32, kind="ExternalInput").ap()
    y1 = nc.dram_tensor("y1", [NCD, 128, T], BF16, kind="ExternalInput").ap()
    xa1 = nc.dram_tensor("xa1", [NCD, 128, T], BF16, kind="ExternalInput").ap()
    out_f = nc.dram_tensor("out_f", [NCD, 128, T], F32, kind="ExternalOutput").ap()

    sel_np = np.zeros((128, 16), dtype=np.float32)
    for p in range(128):
        sel_np[p, p // 8] = 1.0
    selt_np = np.ascontiguousarray(sel_np.T)
    sel_t = nc.inline_tensor(sel_np, "sel_c").ap()
    selt_t = nc.inline_tensor(selt_np, "selt_c").ap()

    with ExitStack() as ctx:
        ssb = _sb(nc, ctx, "ssb", [128, NT_ALL // 8], F32)
        sown = _sb(nc, ctx, "sown", [16, T], F32)
        sel_sb = _sb(nc, ctx, "sel_sb", [128, 16], F32)
        selt_sb = _sb(nc, ctx, "selt_sb", [16, 128], F32)
        ones16 = _sb(nc, ctx, "ones16", [16, 1], F32)
        one32 = _sb(nc, ctx, "one32", [1, 128], F32)
        lo = _sb(nc, ctx, "lo", [16, 1], F32)
        hi = _sb(nc, ctx, "hi", [16, 1], F32)
        mid = _sb(nc, ctx, "mid", [16, 1], F32)
        ge = _sb(nc, ctx, "ge", [16, 1], I32)
        c16 = _sb(nc, ctx, "c16", [16, 1], F32)
        midbc = _sb(nc, ctx, "midbc", [128, 1], F32)
        maskb = _sb(nc, ctx, "maskb", [128, NT_ALL // 8], BF16)
        cnt = _sb(nc, ctx, "cnt", [128, 1], F32)
        wmask = _sb(nc, ctx, "wmask", [16, T], F32)
        wrow = _sb(nc, ctx, "wrow", [1, T], F32)
        wbc = _sb(nc, ctx, "wbc", [128, QB], F32)
        ytile = _sb(nc, ctx, "ytile", [128, QB], BF16)
        xtile = _sb(nc, ctx, "xtile", [128, QB], BF16)
        otile = _sb(nc, ctx, "otile", [128, 2, QB], F32)
        t1 = _sb(nc, ctx, "t1", [128, QB], F32)
        ps0 = _ps(nc, ctx, "ps0", [128, 512])
        ps1 = _ps(nc, ctx, "ps1", [128, 512])
        e = E(nc, ctx)
        blk = ctx.enter_context(nc.Block())

        def g_pre(g):
            e.inc(g.dma_start(ssb[:, :], s_all.rearrange("e (cc t) -> (e cc) t", cc=8)),
                  "ld", 16)
            e.inc(g.dma_start(sown[:, :], s_own), "ld", 16)
            e.inc(g.dma_start(sel_sb[:, :], sel_t), "ld", 16)
            e.inc(g.dma_start(selt_sb[:, :], selt_t), "ld", 16)
            e.inc(g.memset(ones16[:, :], 1.0), "init", 1)
            e.inc(g.memset(one32[:, :], 1.0), "init", 1)
            e.inc(g.memset(lo[:, :], 0.0), "init", 1)
            e.inc(g.memset(hi[:, :], 2.0), "init", 1)
        blk.gpsimd(g_pre)

        NITER = 42
        for it in range(NITER):
            def v_mid(v, it=it):
                if it == 0:
                    e.wait_now(v, "ld", "init")
                else:
                    e.wait_now(v, "upd")
                e.inc(v.tensor_add(mid[:, :], lo[:, :], hi[:, :]), "tkv", 1)
                e.wait_now(v, "tkv")
                e.inc(v.tensor_scalar_mul(mid[:, :], mid[:, :], 0.5), "midd", 1)
            blk.vector(v_mid)

            def t_bc(t, it=it):
                e.wait_now(t, "midd")
                if it > 0:
                    e.wait_now(t, "cntd")
                e.inc(t.matmul(ps0[:, 0:1], selt_sb[:, :], mid[:, :],
                               start=True, stop=True), "mmb", 1)
            blk.tensor(t_bc)

            def v_cnt(v, it=it):
                e.wait_now(v, "mmb")
                e.inc(v.tensor_copy(midbc[:, :], ps0[:, 0:1]), "tkv", 1)
                e.wait_now(v, "tkv")
                e.inc(v.tensor_scalar(maskb[:, :], ssb[:, :], midbc[:, :], None,
                                      ALU.is_ge), "tkv", 1)
                e.wait_now(v, "tkv")
                e.inc(v.reduce_sum(cnt[:, :], maskb[:, :], axis=AX), "cntd", 1)
            blk.vector(v_cnt)

            def t_c16(t, it=it):
                e.wait_now(t, "cntd")
                e.inc(t.matmul(ps1[0:16, 0:1], sel_sb[:, :], cnt[:, :],
                               start=True, stop=True), "mmc", 1)
            blk.tensor(t_c16)

            def v_upd(v, it=it):
                e.wait_now(v, "mmc")
                e.inc(v.tensor_copy(c16[:, :], ps1[0:16, 0:1]), "tkv", 1)
                e.wait_now(v, "tkv")
                e.inc(v.tensor_scalar(ge[:, :], c16[:, :], 1024.0, None, ALU.is_ge),
                      "tkv", 1)
                e.wait_now(v, "tkv")
                e.inc(v.select(lo[:, :], ge[:, :], mid[:, :], lo[:, :]), "tkv", 1)
                e.wait_now(v, "tkv")
                e.inc(v.select(hi[:, :], ge[:, :], hi[:, :], mid[:, :]), "upd", 1)
            blk.vector(v_upd)

        def v_w(v):
            e.wait_now(v, "upd")
            e.inc(v.tensor_scalar(wmask[:, :], sown[:, :], lo[:, :], None, ALU.is_ge),
                  "tkv", 1)
            e.wait_now(v, "tkv")
            e.inc(v.tensor_mul(wmask[:, :], wmask[:, :], sown[:, :]), "wmd", 1)
        blk.vector(v_w)

        def t_w(t):
            e.wait_now(t, "wmd", "init")
            for qb in range(NQB):
                e.inc(t.matmul(ps0[0:1, :], ones16[:, :],
                               wmask[:, qb * QB:(qb + 1) * QB], start=True, stop=True),
                      "mmw", 1)
        blk.tensor(t_w)
        mmw0 = e.v("mmw")

        def t_w2(t, mmw0=mmw0):
            for qb in range(NQB):
                tgt = ps0 if qb % 2 == 0 else ps1
                if qb >= 2:
                    e.wait_now(t, "wcp")
                e.inc(t.matmul(tgt[0:1, :], ones16[:, :],
                               wmask[:, qb * QB:(qb + 1) * QB], start=True, stop=True),
                      "mmw2", 1)
        blk.tensor(t_w2)
        mw2 = e.v("mmw2")

        def v_wrow(v, mw2=mw2):
            for qb in range(NQB):
                v.wait_ge(e.s("mmw2"), mw2 - NQB + qb + 1)
                tgt = ps0 if qb % 2 == 0 else ps1
                e.inc(v.tensor_copy(wrow[:, qb * QB:(qb + 1) * QB], tgt[0:1, :]),
                      "wcp", 1)
        blk.vector(v_wrow)

        for qb in range(NQB):
            def t_wbc(t, qb=qb):
                e.wait_now(t, "wcp", "fdone")
                e.inc(t.matmul(ps0[:, :], one32[:, :], wrow[:, qb * QB:(qb + 1) * QB],
                               start=True, stop=True), "mmwb", 1)
            blk.tensor(t_wbc)

            def v_wbc(v, qb=qb):
                e.wait_now(v, "mmwb")
                e.inc(v.tensor_copy(wbc[:, :], ps0[:, :]), "wbcd", 1)
            blk.vector(v_wbc)

            for dc in range(NCD):
                def g_l(g, dc=dc, qb=qb):
                    e.wait_now(g, "fdone")
                    e.inc(g.dma_start(ytile[:, :], y1[dc][:, qb * QB:(qb + 1) * QB]),
                          "fl", 16)
                    e.inc(g.dma_start(xtile[:, :], xa1[dc][:, qb * QB:(qb + 1) * QB]),
                          "fl", 16)
                blk.gpsimd(g_l)
                flv = e.v("fl")

                def v_f(v, dc=dc, qb=qb, flv=flv):
                    v.wait_ge(e.s("fl"), flv)
                    e.wait_now(v, "wbcd", "fst")
                    e.inc(v.tensor_mul(t1[:, :], ytile[:, :], wbc[:, :]), "tkv", 1)
                    e.wait_now(v, "tkv")
                    mi = v.tensor_add(otile[:, dc % 2, :], t1[:, :], xtile[:, :])
                    e.inc(mi, "fdone", 1)
                blk.vector(v_f)

                def g_s(g, dc=dc, qb=qb):
                    e.wait_now(g, "fdone")
                    e.inc(g.dma_start(out_f[dc][:, qb * QB:(qb + 1) * QB],
                                      otile[:, dc % 2, :]), "fst", 16)
                blk.gpsimd(g_s)

        def g_fin(g):
            g.wait_ge(e.s("fst"), e.v("fst"))
        blk.gpsimd(g_fin)
    return nc




def build_fused():
    nc = bass.Bass("TRN2", target_bir_lowering=False)
    U32 = mybir.dt.uint32
    ap = lambda *a, **k: nc.dram_tensor(*a, **k).ap()
    x_fm = ap("x_fm", [NCD, 128, T], F32, kind="ExternalInput")
    wq_t = ap("wq_t", [NH, D, DH], F32, kind="ExternalInput")
    wk_t = ap("wk_t", [NH, D, DH], F32, kind="ExternalInput")
    wv_t = ap("wv_t", [NH, D, DH], F32, kind="ExternalInput")
    wo_t = ap("wo_t", [NH, DH, D], F32, kind="ExternalInput")
    delta = ap("delta", [128, 128], F32, kind="ExternalInput")
    g_wt = ap("g_wt", [D, IF], I32, kind="ExternalInput")
    u_wt = ap("u_wt", [D, IF], I32, kind="ExternalInput")
    d_wt = ap("d_wt", [IF, D], I32, kind="ExternalInput")
    gsc = ap("gsc", [128, NIC], F32, kind="ExternalInput")
    usc = ap("usc", [128, NIC], F32, kind="ExternalInput")
    dsc = ap("dsc", [128, NCD], F32, kind="ExternalInput")
    wq1 = ap("wq1", [D, D], F32, kind="ExternalInput")
    bq1 = ap("bq1", [1, D], F32, kind="ExternalInput")
    wk1 = ap("wk1", [D, 512], F32, kind="ExternalInput")
    bk1 = ap("bk1", [1, 512], F32, kind="ExternalInput")
    wv1 = ap("wv1", [D, 512], F32, kind="ExternalInput")
    bv1 = ap("bv1", [1, 512], F32, kind="ExternalInput")
    wo1 = ap("wo1", [D, D], F32, kind="ExternalInput")
    bo1 = ap("bo1", [1, D], F32, kind="ExternalInput")
    rw = ap("rw", [D, 16], F32, kind="ExternalInput")
    rb = ap("rb", [1, 16], F32, kind="ExternalInput")
    ew = ap("ew", [D, D], F32, kind="ExternalInput")
    eb = ap("eb", [1, D], F32, kind="ExternalInput")
    rank = ap("rank", [1, 1], U32, kind="ExternalInput")
    nonce = ap("nonce", [1, 1], I32, kind="ExternalInput")
    s_t = ap("s_t", [NTT, 128, 16], F32, kind="ExternalOutput")
    xa1 = ap("xa1", [NCD, 128, T], BF16, kind="ExternalOutput")
    y1 = ap("y1", [NCD, 128, T], BF16, kind="ExternalOutput")
    xn0 = ap("xn0", [NCD, 128, T], BF16)
    s1 = ap("s1", [1, T], F32)
    q0 = ap("q0", [NH, 128, T], BF16)
    qm2_0 = ap("qm2_0", [1, NH], F32)
    xa = ap("xa", [NCD, 128, T], BF16)
    x1 = ap("x1", [NCD, 128, T], BF16)
    xn1 = ap("xn1", [NCD, 128, T], BF16)
    q1 = ap("q1", [NH, 128, T], BF16)
    qm2_1 = ap("qm2_1", [1, NH], F32)
    h_dram = ap("h_dram", [NIC, 128, T], BF16)
    y_dram = ap("y_dram", [NCD, 128, T], F32)
    kf0 = ap("kf0", [2, NH, 128, SK // 2], BF16, addr_space="Shared")
    vf0 = ap("vf0", [2, NTT, 128, NH * DH], BF16, addr_space="Shared")
    km0 = ap("km0", [2, NH], F32, addr_space="Shared")
    kf1 = ap("kf1", [2, NKV, 128, SK // 2], BF16, addr_space="Shared")
    vf1 = ap("vf1", [2, NTT, 128, NKV * DH], BF16, addr_space="Shared")
    km1 = ap("km1", [2, NKV], F32, addr_space="Shared")
    flags = ap("flags", [2, 4], I32, addr_space="Shared")

    with ExitStack() as octx:
        e = E(nc, octx)
        blk = octx.enter_context(nc.Block())
        rank_sb = _sb(nc, octx, "rank_sb", [1, 1], U32)
        nonce_sb = _sb(nc, octx, "nonce_sb", [1, 1], I32)
        fs_sb = _sb(nc, octx, "fs_sb", [1, 1], I32)
        rank_reg = nc.gpsimd.alloc_register("rank_reg")

        def g0(g):
            e.inc(g.dma_start(rank_sb[:, :], rank), "ld", 16)
            e.inc(g.dma_start(nonce_sb[:, :], nonce), "ld", 16)
            e.wait_now(g, "ld")
            g.load(rank_reg, rank_sb[0:1, 0:1])
        blk.gpsimd(g0)

        kvsh0 = dict(kf=kf0, vf=vf0, km2=km0, flags=flags, fcol=0,
                     rank_reg=rank_reg, nonce_sb=nonce_sb, fs_sb=fs_sb)
        kvsh1 = dict(kf=kf1, vf=vf1, km2=km1, flags=flags, fcol=1,
                     rank_reg=rank_reg, nonce_sb=nonce_sb, fs_sb=fs_sb)

        build_ln_qkv0(fused=(nc, blk, e, dict(
            x_fm=x_fm, wq_t=wq_t, wk_t=wk_t, wv_t=wv_t,
            xn0=xn0, s1=s1, q0=q0, qm2=qm2_0, kvsh=kvsh0)))
        phase_barrier(blk, e)
        build_attn(0, fused=(nc, blk, e, dict(
            q_in=q0, kf=kf0, vf=vf0, qm2=qm2_0, km2=km0, x_res=x_fm,
            wo_in=wo_t, delta=delta, xa=xa, kvsh=kvsh0)))
        phase_barrier(blk, e)
        build_ffn0(fused=(nc, blk, e, dict(
            xn_in=xn0, s1=s1, g_wt=g_wt, u_wt=u_wt, d_wt=d_wt,
            gsc=gsc, usc=usc, dsc=dsc, xa=xa, x1=x1,
            h_dram=h_dram, y_dram=y_dram)))
        phase_barrier(blk, e)
        build_ln_qkv1(fused=(nc, blk, e, dict(
            x_in=x1, wq=wq1, bq=bq1, wk=wk1, bk=bk1, wv=wv1, bv=bv1,
            rw=rw, rb=rb, xn1=xn1, q1=q1, s_t=s_t, qm2=qm2_1, kvsh=kvsh1)))
        phase_barrier(blk, e)
        build_attn(1, fused=(nc, blk, e, dict(
            q_in=q1, kf=kf1, vf=vf1, qm2=qm2_1, km2=km1, x_res=x1,
            wo_in=wo1, bo=bo1, xa=xa1, kvsh=kvsh1)))
        phase_barrier(blk, e)
        build_expert(fused=(nc, blk, e, dict(xn_in=xn1, ew=ew, eb=eb, y1=y1)))
        phase_barrier(blk, e)
    return nc



_NC_CACHE = {}


def _get(name, builder, *a):
    key = (name,) + a
    if key not in _NC_CACHE:
        _NC_CACHE[key] = builder(*a)
    return _NC_CACHE[key]


FUSED = True


def _stage_host(inp):
    f32 = np.float32
    st = {}
    x = np.asarray(inp["L0_attn_x"], dtype=f32)
    st["wq_t"] = np.ascontiguousarray(np.asarray(inp["L0_attn_W_Q"], f32).transpose(0, 2, 1))
    st["wk_t"] = np.ascontiguousarray(np.asarray(inp["L0_attn_W_K"], f32).transpose(0, 2, 1))
    st["wv_t"] = np.ascontiguousarray(np.asarray(inp["L0_attn_W_V"], f32).transpose(0, 2, 1))
    st["wo_t"] = np.ascontiguousarray(np.asarray(inp["L0_attn_W_O"], f32).transpose(0, 2, 1))
    st["g_wt"] = np.ascontiguousarray(np.asarray(inp["L0_ffn_gate_w"], np.int32).T)
    st["u_wt"] = np.ascontiguousarray(np.asarray(inp["L0_ffn_up_w"], np.int32).T)
    st["d_wt"] = np.ascontiguousarray(np.asarray(inp["L0_ffn_down_w"], np.int32).T)
    st["gsc"] = np.ascontiguousarray(np.asarray(inp["L0_ffn_gate_scale"], f32).reshape(NIC, 128).T)
    st["usc"] = np.ascontiguousarray(np.asarray(inp["L0_ffn_up_scale"], f32).reshape(NIC, 128).T)
    st["dsc"] = np.ascontiguousarray(np.asarray(inp["L0_ffn_down_scale"], f32).reshape(NCD, 128).T)
    st["wq1"] = np.ascontiguousarray(np.asarray(inp["L1_attn_wq"], f32))
    st["bq1"] = np.asarray(inp["L1_attn_bq"], f32).reshape(1, D)
    st["wk1"] = np.ascontiguousarray(np.asarray(inp["L1_attn_wk"], f32))
    st["bk1"] = np.asarray(inp["L1_attn_bk"], f32).reshape(1, 512)
    st["wv1"] = np.ascontiguousarray(np.asarray(inp["L1_attn_wv"], f32))
    st["bv1"] = np.asarray(inp["L1_attn_bv"], f32).reshape(1, 512)
    st["wo1"] = np.ascontiguousarray(np.asarray(inp["L1_attn_wo"], f32))
    st["bo1"] = np.asarray(inp["L1_attn_bo"], f32).reshape(1, D)
    st["rw"] = np.ascontiguousarray(np.asarray(inp["L1_ffn_router_w"], f32))
    st["rb"] = np.asarray(inp["L1_ffn_router_b"], f32).reshape(1, 16)
    st["ew"] = np.ascontiguousarray(np.asarray(inp["L1_ffn_expert_w"], f32))
    st["eb"] = np.asarray(inp["L1_ffn_expert_b"], f32).reshape(1, D)
    x_fm, deltas = [], []
    for c in range(8):
        p, r = c // 2, c % 2
        xs = x[p].reshape(32, 128, D)[r::2].reshape(T, D)
        x_fm.append(np.ascontiguousarray(xs.T).reshape(NCD, 128, T))
        thr = np.empty(128, f32)
        for qb in range(NQB):
            for kt in range(32):
                thr[qb * 32 + kt] = -(1024 * qb + 128 * r - 128 * kt) - 1
        deltas.append(np.tile(thr[None, :], (128, 1)).astype(f32))
    st["x_fm"] = x_fm
    st["deltas"] = deltas
    return st


_NONCE = [1000]


def kernel(**inp):
    cores = list(range(8))
    st = _stage_host(inp)
    x_fm, deltas = st["x_fm"], st["deltas"]

    if FUSED:
        _NONCE[0] += 1
        nonce_val = np.array([[int(np.random.randint(1, 2**30)) + _NONCE[0]]],
                             dtype=np.int32)
        ncA = _get("fA", build_fused)
        imA = [{"x_fm": x_fm[c], "wq_t": st["wq_t"], "wk_t": st["wk_t"],
                "wv_t": st["wv_t"], "wo_t": st["wo_t"], "delta": deltas[c],
                "g_wt": st["g_wt"], "u_wt": st["u_wt"], "d_wt": st["d_wt"],
                "gsc": st["gsc"], "usc": st["usc"], "dsc": st["dsc"],
                "wq1": st["wq1"], "bq1": st["bq1"], "wk1": st["wk1"],
                "bk1": st["bk1"], "wv1": st["wv1"], "bv1": st["bv1"],
                "wo1": st["wo1"], "bo1": st["bo1"], "rw": st["rw"],
                "rb": st["rb"], "ew": st["ew"], "eb": st["eb"],
                "rank": np.array([[c % 2]], dtype=np.uint32),
                "nonce": nonce_val} for c in cores]
        rA = run_bass_kernel_spmd(ncA, imA, core_ids=cores).results
        s_ownT = []
        for c in cores:
            stt = rA[c]["s_t"].reshape(T, 16)
            s_ownT.append(np.ascontiguousarray(stt.T))
        s_all = np.ascontiguousarray(np.concatenate(s_ownT, axis=1))
        ncB = _get("l7", build_moe_final)
        imB = [{"s_all": s_all, "s_own": s_ownT[c], "y1": rA[c]["y1"],
                "xa1": rA[c]["xa1"]} for c in cores]
        rB = run_bass_kernel_spmd(ncB, imB, core_ids=cores).results
        out = np.empty((4, 4096, D), dtype=np.float32)
        for c in cores:
            p, r = c // 2, c % 2
            fm = rB[c]["out_f"].reshape(D, T)
            out[p].reshape(32, 128, D)[r::2] = fm.T.reshape(16, 128, D)
        return out

    nc1 = _get("l1", build_ln_qkv0)
    im1 = [{"x_fm": x_fm[c], "wq_t": st["wq_t"], "wk_t": st["wk_t"],
            "wv_t": st["wv_t"]} for c in cores]
    r1 = run_bass_kernel_spmd(nc1, im1, core_ids=cores).results

    kf, vfa, km2p = {}, {}, {}
    for p in range(4):
        ke, ko = r1[2 * p]["k0"], r1[2 * p + 1]["k0"]
        kfp = np.empty((NH, 128, 32, 128), dtype=BF)
        kfp[:, :, 0::2, :] = ke.reshape(NH, 128, 16, 128)
        kfp[:, :, 1::2, :] = ko.reshape(NH, 128, 16, 128)
        kf[p] = np.ascontiguousarray(kfp.reshape(NH, 128, SK))
        ve, vo_ = r1[2 * p]["v0"], r1[2 * p + 1]["v0"]
        vfp = np.empty((32, 128, NH * DH), dtype=BF)
        vfp[0::2] = ve.transpose(0, 2, 1, 3).reshape(16, 128, NH * DH)
        vfp[1::2] = vo_.transpose(0, 2, 1, 3).reshape(16, 128, NH * DH)
        vfa[p] = np.ascontiguousarray(vfp)
        km2p[p] = np.concatenate([r1[2 * p]["km2"], r1[2 * p + 1]["km2"]], axis=0)

    nc2 = _get("l2", build_attn, 0)
    im2 = [{"q_in": r1[c]["q0"], "kf": kf[c // 2], "vf": vfa[c // 2],
            "qm2": r1[c]["qm2"], "km2": km2p[c // 2], "x_res": x_fm[c],
            "wo_in": st["wo_t"], "delta": deltas[c]} for c in cores]
    r2 = run_bass_kernel_spmd(nc2, im2, core_ids=cores).results

    nc3 = _get("l3", build_ffn0)
    im3 = [{"xn_in": r1[c]["xn0"], "s1": r1[c]["s1"], "g_wt": st["g_wt"],
            "u_wt": st["u_wt"], "d_wt": st["d_wt"], "gsc": st["gsc"],
            "usc": st["usc"], "dsc": st["dsc"], "xa": r2[c]["xa"]}
           for c in cores]
    r3 = run_bass_kernel_spmd(nc3, im3, core_ids=cores).results

    nc4 = _get("l4", build_ln_qkv1)
    im4 = [{"x_in": r3[c]["x1"], "wq": st["wq1"], "bq": st["bq1"],
            "wk": st["wk1"], "bk": st["bk1"], "wv": st["wv1"], "bv": st["bv1"],
            "rw": st["rw"], "rb": st["rb"]} for c in cores]
    r4 = run_bass_kernel_spmd(nc4, im4, core_ids=cores).results

    kf1, vf1, km2p1 = {}, {}, {}
    for p in range(4):
        ke, ko = r4[2 * p]["k1"], r4[2 * p + 1]["k1"]
        kfp = np.empty((NKV, 128, 32, 128), dtype=BF)
        kfp[:, :, 0::2, :] = ke.reshape(NKV, 128, 16, 128)
        kfp[:, :, 1::2, :] = ko.reshape(NKV, 128, 16, 128)
        kf1[p] = np.ascontiguousarray(kfp.reshape(NKV, 128, SK))
        vfp = np.empty((32, 128, NKV * DH), dtype=BF)
        vfp[0::2] = r4[2 * p]["v1"]
        vfp[1::2] = r4[2 * p + 1]["v1"]
        vf1[p] = np.ascontiguousarray(vfp)
        km2p1[p] = np.concatenate([r4[2 * p]["km2"], r4[2 * p + 1]["km2"]], axis=0)

    nc5 = _get("l5", build_attn, 1)
    im5 = [{"q_in": r4[c]["q1"], "kf": kf1[c // 2], "vf": vf1[c // 2],
            "qm2": r4[c]["qm2"], "km2": km2p1[c // 2], "x_res": r3[c]["x1"],
            "wo_in": st["wo1"], "bo": st["bo1"]} for c in cores]
    r5 = run_bass_kernel_spmd(nc5, im5, core_ids=cores).results

    nc6 = _get("l6", build_expert)
    im6 = [{"xn_in": r4[c]["xn1"], "ew": st["ew"], "eb": st["eb"]} for c in cores]
    r6 = run_bass_kernel_spmd(nc6, im6, core_ids=cores).results

    s_ownT = []
    for c in cores:
        stt = r4[c]["s_t"].reshape(T, 16)
        s_ownT.append(np.ascontiguousarray(stt.T))
    s_all = np.ascontiguousarray(np.concatenate(s_ownT, axis=1))

    nc7 = _get("l7", build_moe_final)
    im7 = [{"s_all": s_all, "s_own": s_ownT[c], "y1": r6[c]["y1"],
            "xa1": r5[c]["xa"]} for c in cores]
    r7 = run_bass_kernel_spmd(nc7, im7, core_ids=cores).results

    out = np.empty((4, 4096, D), dtype=np.float32)
    for c in cores:
        p, r = c // 2, c % 2
        fm = r7[c]["out_f"].reshape(D, T)
        out[p].reshape(32, 128, D)[r::2] = fm.T.reshape(16, 128, D)
    return out
